# version 01 (final)
# baseline (speedup 1.0000x reference)
import sys
sys.path.insert(0, '/opt/trn_rl_repo')
from contextlib import ExitStack
import numpy as np
import concourse.bass as bass
import concourse.mybir as mybir
from concourse.bass_utils import run_bass_kernel_spmd

F32 = mybir.dt.float32
BF16 = mybir.dt.bfloat16
AF = mybir.ActivationFunctionType
ALU = mybir.AluOpType
AX = mybir.AxisListType

B, L, D = 4, 4096, 1024
EPS_DEN = 1e-6
LN_EPS = 1e-5


def sb(st, nc, name, shape, dtype):
    return st.enter_context(nc.sbuf_tensor(name, shape, dtype))


def ps(st, nc, name, shape, dtype):
    return st.enter_context(nc.psum_tensor(name, shape, dtype))


def sem(st, nc, name):
    return st.enter_context(nc.semaphore(name))


def build_A(dbg=False):
    nc = bass.Bass("TRN2", target_bir_lowering=False, debug=False)
    xb = nc.dram_tensor("xb", [L, D], F32, kind="ExternalInput").ap()
    wqkv_d = nc.dram_tensor("wqkv", [D, 1536], F32, kind="ExternalInput").ap()
    brow_d = nc.dram_tensor("brow", [1, 1536], F32, kind="ExternalInput").ap()
    ola = nc.dram_tensor("ola", [512, L], F32, kind="ExternalOutput").ap()
    if dbg:
        dxnT = nc.dram_tensor("dxnT", [128, 2048], BF16, kind="ExternalOutput").ap()
        dqf = nc.dram_tensor("dqf", [128, 2048], BF16, kind="ExternalOutput").ap()
        dkf = nc.dram_tensor("dkf", [128, 2048], BF16, kind="ExternalOutput").ap()
        dkt = nc.dram_tensor("dkt", [128, 512], BF16, kind="ExternalOutput").ap()
        dvt = nc.dram_tensor("dvt", [128, 512], BF16, kind="ExternalOutput").ap()
        dKV = nc.dram_tensor("dKV", [128, 256], F32, kind="ExternalOutput").ap()
        dks = nc.dram_tensor("dks", [512, 1], F32, kind="ExternalOutput").ap()
        dstm = nc.dram_tensor("dstm", [128, 128], F32, kind="ExternalOutput").ap()
        dnum = nc.dram_tensor("dnum", [128, 128], F32, kind="ExternalOutput").ap()
        dden = nc.dram_tensor("dden", [1, 128], F32, kind="ExternalOutput").ap()
        dkvb = nc.dram_tensor("dkvb", [128, 256], F32, kind="ExternalOutput").ap()

    NP, NT = (1 if dbg else 2), 16
    NW = 16 * 9
    st = ExitStack()
    with st:
        xt = [sb(st, nc, f"xt{i}", [128, D], F32) for i in range(2)]
        xnf = [sb(st, nc, f"xnf{i}", [128, D], F32) for i in range(2)]
        xnb = [sb(st, nc, f"xnb{i}", [128, D], BF16) for i in range(2)]
        ln_sum = [sb(st, nc, f"ln_sum{i}", [128, 1], F32) for i in range(2)]
        ln_nm = [sb(st, nc, f"ln_nm{i}", [128, 1], F32) for i in range(2)]
        ln_rt = [sb(st, nc, f"ln_rt{i}", [128, 1], F32) for i in range(2)]
        ln_rs = [sb(st, nc, f"ln_rs{i}", [128, 1], F32) for i in range(2)]
        sqs = sb(st, nc, "sqs", [128, D], BF16)
        xnT = [sb(st, nc, f"xnT{i}", [128, 2048], BF16) for i in range(8)]
        wqs = sb(st, nc, "wqs", [128, 1536], F32)
        wqb = [sb(st, nc, f"wqb{i}", [128, 1536], BF16) for i in range(8)]
        bb = sb(st, nc, "bb", [1, 1536], BF16)
        ones_row = sb(st, nc, "ones_row", [1, 512], BF16)
        ones_col = sb(st, nc, "ones_col", [128, 1], BF16)
        ones_tile = sb(st, nc, "ones_tile", [128, 128], BF16)
        iden = sb(st, nc, "iden", [128, 128], BF16)
        epsln = sb(st, nc, "epsln", [128, 1], F32)
        qf = [sb(st, nc, f"qf{i}", [128, 2048], BF16) for i in range(4)]
        kf = [sb(st, nc, f"kf{i}", [128, 2048], BF16) for i in range(4)]
        kt = [sb(st, nc, f"kt{i}", [128, 512], BF16) for i in range(16)]
        vt = [sb(st, nc, f"vt{i}", [128, 512], BF16) for i in range(16)]
        fmr = [sb(st, nc, f"fmr{i}", [128, 512], F32) for i in range(2)]
        fmm = [sb(st, nc, f"fmm{i}", [128, 512], F32) for i in range(2)]
        fme = [sb(st, nc, f"fme{i}", [128, 512], F32) for i in range(2)]
        stm0 = sb(st, nc, "stm0", [128, 128], BF16)
        stm = sb(st, nc, "stm", [128, 128], BF16)
        KVh = sb(st, nc, "KVh", [128, 4 * 64], F32)
        KVb = sb(st, nc, "KVb", [128, 4 * 64], BF16)
        ksum = [sb(st, nc, f"ksum{i}", [128, 1], F32) for i in range(4)]
        ksb = [sb(st, nc, f"ksb{i}", [128, 1], BF16) for i in range(4)]
        ksd = sb(st, nc, "ksd", [128, 1], F32)
        dens = sb(st, nc, "dens", [1, 128], F32)
        recd = sb(st, nc, "recd", [1, 128], F32)
        denb = sb(st, nc, "denb", [1, 128], BF16)
        numev = sb(st, nc, "numev", [128, 128], F32)
        osb = [sb(st, nc, f"osb{i}", [128, 2048], F32) for i in range(4)]

        if dbg:
            dstm_sb = sb(st, nc, "dstm_sb", [128, 128], F32)
            dnum_sb = sb(st, nc, "dnum_sb", [128, 128], F32)
            dden_sb = sb(st, nc, "dden_sb", [1, 128], F32)
            dkvb_sb = sb(st, nc, "dkvb_sb", [128, 256], F32)

        ptrT = ps(st, nc, "ptrT", [128, 1024], BF16)
        pq = [ps(st, nc, f"pq{i}", [128, 512], F32) for i in range(2)]
        pst = ps(st, nc, "pst", [128, 128], F32)
        pnum = ps(st, nc, "pnum", [128, 128], F32)
        pden = ps(st, nc, "pden", [1, 128], F32)
        pkv = ps(st, nc, "pkv", [128, 64], F32)
        pbc = ps(st, nc, "pbc", [128, 128], F32)

        sld = sem(st, nc, "sld")
        sv = sem(st, nc, "sv")
        ss = sem(st, nc, "ss")
        stn = sem(st, nc, "stn")
        sgp = sem(st, nc, "sgp")
        sst = sem(st, nc, "sst")
        svr = sem(st, nc, "svr")

        A2K = 64
        def a2_vincs(k):
            return 2 if k < 48 else 1
        V_A2 = sum(a2_vincs(k) for k in range(A2K))
        V_PASS = 3 * NT + V_A2 + 3 * 128

        def v_xc(p, tt):    return 1 + V_PASS * p + 3 * tt + 1
        def v_xnb(p, tt):   return 1 + V_PASS * p + 3 * tt + 2
        def v_xnT(p, tt):   return 1 + V_PASS * p + 3 * tt + 3
        def v_ln_all(p):    return 1 + V_PASS * p + 3 * NT
        def v_a2(p, k):     return v_ln_all(p) + sum(a2_vincs(j) for j in range(k + 1))
        def v_a2_all(p):    return v_ln_all(p) + V_A2
        def v_evac(p, s):   return v_a2_all(p) + 3 * s + 1
        def v_state(p, s):  return v_a2_all(p) + 3 * s + 2
        def v_out(p, s):    return v_a2_all(p) + 3 * s + 3

        S_PASS = NT + 48
        def s_ln(p, tt):    return S_PASS * p + tt + 1
        def s_a2(p, k):     return S_PASS * p + NT + k + 1

        T_PASS = NT + A2K + 3 * 128
        def t_tr(p, tt):    return T_PASS * p + tt + 1
        def t_a2(p, k):     return T_PASS * p + NT + k + 1
        def t_st(p, s):     return T_PASS * p + NT + A2K + 3 * s + 1
        def t_num(p, s):    return T_PASS * p + NT + A2K + 3 * s + 2
        def t_bc(p, s):     return T_PASS * p + NT + A2K + 3 * s + 3

        def g_wc(dc):       return 2 + dc + 1
        G_BIAS = 11
        def g_mask(p, s):   return G_BIAS + 128 * p + s + 1

        @block_holder(nc)
        def build(block):
            @block.sync
            def _(sync):
                for dc in range(8):
                    if dc > 0:
                        sync.wait_ge(sgp, g_wc(dc - 1))
                    sync.dma_start(out=wqs[:, :], in_=wqkv_d[dc * 128:(dc + 1) * 128, :]).then_inc(sld, 16)
                sync.wait_ge(sgp, g_wc(7))
                sync.dma_start(out=wqs[0:1, :], in_=brow_d).then_inc(sld, 16)
                for p in range(NP):
                    for tt in range(NT):
                        gi = p * NT + tt
                        if gi >= 2:
                            g2 = gi - 2
                            sync.wait_ge(sv, v_xnb(g2 // NT, g2 % NT))
                        sync.dma_start(out=xt[gi % 2][:, :], in_=xb[gi * 128:(gi + 1) * 128, :]).then_inc(sld, 16)

            @block.gpsimd
            def _(g):
                g.memset(ones_tile[:], 1.0)
                g.memset(ones_row[:], 1.0)
                g.memset(ones_col[:], 1.0)
                g.affine_select(iden[:], ones_tile[:], [[1, 128]], ALU.is_equal, 0.0,
                                base=0, channel_multiplier=-1).then_inc(sgp, 1)
                g.memset(KVh[:], 0.0)
                g.memset(KVb[:], 0.0)
                for c4 in range(4):
                    g.memset(ksum[c4][:], 0.0)
                    g.memset(ksb[c4][:], 0.0)
                g.engine_nop().then_inc(sgp, 1)
                for dc in range(8):
                    g.wait_ge(sld, 16 * (dc + 1))
                    g.tensor_copy(wqb[dc][:, :], wqs[:, :]).then_inc(sgp, 1)
                g.wait_ge(sld, NW)
                g.tensor_copy(bb[:, :], wqs[0:1, :]).then_inc(sgp, 1)
                for p in range(NP):
                    for s in range(128):
                        g.wait_ge(sv, v_evac(p, s))
                        if s > 0:
                            g.wait_ge(stn, t_num(p, s - 1))
                        elif p == 1:
                            g.wait_ge(stn, t_num(0, 127))
                        g.affine_select(stm[:], stm0[:], [[1, 128]], ALU.is_ge, 0.0,
                                        base=0, channel_multiplier=-1).then_inc(sgp, 1)
                        if dbg and p == 0 and s == 1:
                            g.tensor_copy(dstm_sb[:, :], stm[:, :])

            @block.vector
            def _(v):
                nvr = [0]

                def vguard(ins):
                    ins.then_inc(svr, 1)
                    nvr[0] += 1
                    v.wait_ge(svr, nvr[0])

                def vreduce(*args, **kw):
                    vguard(v.tensor_reduce(*args, **kw))

                v.memset(epsln[:], LN_EPS).then_inc(sv, 1)
                for p in range(NP):
                    for tt in range(NT):
                        gi = p * NT + tt
                        bi = gi % 2
                        v.wait_ge(sld, NW + 16 * (gi + 1))
                        vreduce(ln_sum[bi][:, :], xt[bi][:, :], axis=AX.X, op=ALU.add)
                        vguard(v.tensor_scalar(ln_nm[bi][:, :], ln_sum[bi][:, :], -1.0 / D, None, ALU.mult))
                        v.tensor_scalar(xnf[bi][:, :], xt[bi][:, :], ln_nm[bi][:, :], None, ALU.add).then_inc(sv, 1)
                        v.wait_ge(ss, s_ln(p, tt))
                        vguard(v.reciprocal(ln_rs[bi][:, :], ln_rt[bi][:, :]))
                        v.tensor_scalar(xnb[bi][:, :], xnf[bi][:, :], ln_rs[bi][:, :], None, ALU.mult).then_inc(sv, 1)
                        v.wait_ge(stn, t_tr(p, tt))
                        for dc in range(8):
                            ins = v.tensor_copy(xnT[dc][:, tt * 128:(tt + 1) * 128],
                                                ptrT[:, dc * 128:(dc + 1) * 128])
                        ins.then_inc(sv, 1)
                    for k in range(A2K):
                        v.wait_ge(stn, t_a2(p, k))
                        bi = k % 2
                        if k < 32:
                            sc = 0.125 if k < 16 else 1.0
                            oc, strip = (k % 16) // 4, k % 4
                            dst = (qf if k < 16 else kf)[oc][:, strip * 512:(strip + 1) * 512]
                            v.tensor_scalar(fmr[bi][:, :], pq[bi][:, :], sc, 0.0, ALU.mult, ALU.max)
                            v.tensor_scalar(fmm[bi][:, :], pq[bi][:, :], sc, 0.0, ALU.mult, ALU.min).then_inc(sv, 1)
                            v.wait_ge(ss, s_a2(p, k))
                            v.scalar_tensor_tensor(dst, fmr[bi][:, :], 1e-4, fme[bi][:, :],
                                                   ALU.add, ALU.add).then_inc(sv, 1)
                        elif k < 48:
                            tt = k - 32
                            v.tensor_scalar(fmr[bi][:, :], pq[bi][:, :], 1.0, 0.0, ALU.mult, ALU.max)
                            v.tensor_scalar(fmm[bi][:, :], pq[bi][:, :], 1.0, 0.0, ALU.mult, ALU.min).then_inc(sv, 1)
                            v.wait_ge(ss, s_a2(p, k))
                            v.scalar_tensor_tensor(kt[tt][:, :], fmr[bi][:, :], 1e-4, fme[bi][:, :],
                                                   ALU.add, ALU.add).then_inc(sv, 1)
                        else:
                            tt = k - 48
                            v.tensor_copy(vt[tt][:, :], pq[bi][:, :]).then_inc(sv, 1)
                    if p == 1:
                        v.wait_ge(sst, 16 * 4)
                    for h in range(8):
                        c, off = h // 2, (h % 2) * 64
                        sl = slice(off, off + 64)
                        for tt in range(NT):
                            s_ = h * NT + tt
                            v.wait_ge(stn, t_st(p, s_))
                            if dbg and p == 0 and h == 0 and tt == 1:
                                v.tensor_copy(dkvb_sb[:, :], KVb[:, :])
                            v.tensor_copy(stm0[:, :], pst[:, :]).then_inc(sv, 1)
                            v.wait_ge(stn, t_num(p, s_))
                            kcol = KVh[sl, c * 64:(c + 1) * 64]
                            vguard(v.tensor_tensor(kcol, kcol, pkv[sl, :], ALU.add))
                            v.tensor_copy(KVb[sl, c * 64:(c + 1) * 64], kcol)
                            v.tensor_reduce(ksd[sl, :], kf[c][sl, tt * 128:(tt + 1) * 128],
                                            axis=AX.X, op=ALU.add)
                            vguard(v.tensor_scalar(dens[:, :], pden[:, :], EPS_DEN, None, ALU.add))
                            vguard(v.reciprocal(recd[:, :], dens[:, :]))
                            v.tensor_copy(denb[:, :], recd[:, :])
                            vguard(v.tensor_tensor(ksum[c][sl, :], ksum[c][sl, :], ksd[sl, :], ALU.add))
                            if dbg and p == 0 and h == 0 and tt == 1:
                                v.tensor_copy(dden_sb[:, :], dens[:, :])
                            v.tensor_copy(ksb[c][sl, :], ksum[c][sl, :]).then_inc(sv, 1)
                            v.wait_ge(stn, t_bc(p, s_))
                            vguard(v.tensor_copy(numev[sl, :], pnum[sl, :]))
                            if dbg and p == 0 and h == 0 and tt == 1:
                                v.tensor_copy(dnum_sb[:, :], numev[:, :])
                            v.tensor_tensor(osb[c][sl, tt * 128:(tt + 1) * 128],
                                            numev[sl, :], pbc[sl, :], ALU.mult).then_inc(sv, 1)

            @block.scalar
            def _(s):
                for p in range(NP):
                    for tt in range(NT):
                        bi = (p * NT + tt) % 2
                        s.wait_ge(sv, v_xc(p, tt))
                        s.activation(sqs[:, :], xnf[bi][:, :], AF.Square, accum_out=ln_sum[bi][:, :])
                        s.activation(ln_rt[bi][:, :], ln_sum[bi][:, :], AF.Sqrt,
                                     scale=1.0 / D, bias=epsln[:, :]).then_inc(ss, 1)
                    for k in range(48):
                        bi = k % 2
                        s.wait_ge(sv, v_a2(p, k) - 1)
                        s.activation(fme[bi][:, :], fmm[bi][:, :], AF.Exp).then_inc(ss, 1)
                    for c in range(4):
                        s.wait_ge(sv, v_out(p, 127))
                        s.dma_start(out=ola[c * 128:(c + 1) * 128, p * 2048:(p + 1) * 2048],
                                    in_=osb[c][:, :]).then_inc(sst, 16)
                nst = 16 * 4 * NP
                if dbg:
                    for src, dst in [(xnT[0], dxnT), (qf[0], dqf), (kf[0], dkf),
                                     (kt[0], dkt), (vt[0], dvt), (KVh, dKV),
                                     (dstm_sb, dstm), (dnum_sb, dnum), (dden_sb, dden),
                                     (dkvb_sb, dkvb)]:
                        s.dma_start(out=dst, in_=src[:, :]).then_inc(sst, 16)
                    for c4 in range(4):
                        s.dma_start(out=dks[c4 * 128:(c4 + 1) * 128, :], in_=ksum[c4][:, :]).then_inc(sst, 16)
                    nst += 16 * 14
                s.wait_ge(sst, nst)

            @block.tensor
            def _(t):
                t.wait_ge(sgp, G_BIAS)
                for p in range(NP):
                    for tt in range(NT):
                        gi = p * NT + tt
                        t.wait_ge(sv, v_xnb(p, tt))
                        if gi >= 1:
                            g2 = gi - 1
                            t.wait_ge(sv, v_xnT(g2 // NT, g2 % NT))
                        for dc in range(8):
                            ins = t.transpose(ptrT[:, dc * 128:(dc + 1) * 128],
                                              xnb[gi % 2][:, dc * 128:(dc + 1) * 128], iden[:, :])
                        ins.then_inc(stn, 1)
                    for k in range(A2K):
                        bi = k % 2
                        if k == 0:
                            t.wait_ge(sv, v_ln_all(p))
                        if k >= 2:
                            t.wait_ge(sv, v_a2(p, k - 2))
                        if k < 32:
                            oc, strip = k % 16 // 4 + (0 if k < 16 else 4), k % 4
                            for dc in range(8):
                                t.matmul(pq[bi][:, :], wqb[dc][:, oc * 128:(oc + 1) * 128],
                                         xnT[dc][:, strip * 512:(strip + 1) * 512],
                                         start=(dc == 0), stop=False)
                            t.matmul(pq[bi][:, :], bb[:, oc * 128:(oc + 1) * 128], ones_row[:, :],
                                     start=False, stop=True).then_inc(stn, 1)
                        else:
                            tt = k % 16
                            colr = slice(512, 1024) if k < 48 else slice(1024, 1536)
                            for dc in range(8):
                                t.matmul(pq[bi][:, :], xnT[dc][:, tt * 128:(tt + 1) * 128],
                                         wqb[dc][:, colr], start=(dc == 0), stop=False)
                            t.matmul(pq[bi][:, :], ones_row[0:1, 0:128], bb[:, colr],
                                     start=False, stop=True).then_inc(stn, 1)
                    for h in range(8):
                        c, off = h // 2, (h % 2) * 64
                        sl = slice(off, off + 64)
                        for tt in range(NT):
                            s_ = h * NT + tt
                            if s_ == 0:
                                t.wait_ge(sv, v_a2_all(p))
                            if s_ > 0:
                                t.wait_ge(sv, v_evac(p, s_ - 1))
                                t.wait_ge(sgp, g_mask(p, s_ - 1))
                            elif p == 1:
                                t.wait_ge(sv, v_evac(0, 127))
                                t.wait_ge(sgp, g_mask(0, 127))
                            t.matmul(pst[:, :], kf[c][sl, tt * 128:(tt + 1) * 128],
                                     qf[c][sl, tt * 128:(tt + 1) * 128],
                                     start=True, stop=True).then_inc(stn, 1)
                            t.wait_ge(sgp, g_mask(p, s_))
                            if s_ > 0:
                                t.wait_ge(sv, v_out(p, s_ - 1))
                            elif p == 1:
                                t.wait_ge(sv, v_out(0, 127))
                            t.matmul(pnum[sl, :], vt[tt][:, h * 64:(h + 1) * 64], stm[:, :],
                                     start=True, stop=False)
                            t.matmul(pnum[sl, :], KVb[sl, c * 64:(c + 1) * 64],
                                     qf[c][sl, tt * 128:(tt + 1) * 128],
                                     start=False, stop=True)
                            t.matmul(pden[:, :], ones_col[:, :], stm[:, :],
                                     start=True, stop=False)
                            t.matmul(pden[:, :], ksb[c][sl, :],
                                     qf[c][sl, tt * 128:(tt + 1) * 128],
                                     start=False, stop=True)
                            t.matmul(pkv[sl, :], kt[tt][:, h * 64:(h + 1) * 64],
                                     vt[tt][:, h * 64:(h + 1) * 64],
                                     start=True, stop=True).then_inc(stn, 1)
                            t.wait_ge(sv, v_state(p, s_))
                            if s_ > 0:
                                t.wait_ge(sv, v_out(p, s_ - 1))
                            elif p == 1:
                                t.wait_ge(sv, v_out(0, 127))
                            t.matmul(pbc[sl, :], ones_row[0:1, 0:64], denb[:, :],
                                     start=True, stop=True).then_inc(stn, 1)
    return nc


class block_holder:
    def __init__(self, nc):
        self.nc = nc

    def __call__(self, fn):
        with self.nc.Block() as block:
            fn(block)
        return None


def build_B():
    nc = bass.Bass("TRN2", target_bir_lowering=False, debug=False)
    xr_d = nc.dram_tensor("xr", [2048, D], F32, kind="ExternalInput").ap()
    olar_d = nc.dram_tensor("olar", [D, 2048], F32, kind="ExternalInput").ap()
    wout_d = nc.dram_tensor("wout", [D, D], F32, kind="ExternalInput").ap()
    bout_d = nc.dram_tensor("bout", [1, D], F32, kind="ExternalInput").ap()
    wg_d = nc.dram_tensor("wg", [D, 8], F32, kind="ExternalInput").ap()
    rb_d = nc.dram_tensor("rb", [1, 8], F32, kind="ExternalInput").ap()
    w1_d = nc.dram_tensor("w1", [9, 8, 128, 2816], F32, kind="ExternalInput").ap()
    w2_d = nc.dram_tensor("w2", [9, 8, 128, 2816], F32, kind="ExternalInput").ap()
    w3_d = nc.dram_tensor("w3", [9, 22, 128, 1024], F32, kind="ExternalInput").ap()
    x1_d = nc.dram_tensor("x1", [2048, D], F32, kind="ExternalOutput").ap()

    NPASS, NTT = 2, 8
    NE, NHALF, NHC = 9, 2, 11

    st = ExitStack()
    with st:
        acc = [sb(st, nc, f"acc{i}", [128, D], F32) for i in range(NTT)]
        xnf = [sb(st, nc, f"xnf{i}", [128, D], F32) for i in range(2)]
        xnb = [sb(st, nc, f"xnb{i}", [128, D], BF16) for i in range(2)]
        ln_sum = [sb(st, nc, f"ln_sum{i}", [128, 1], F32) for i in range(2)]
        ln_nm = [sb(st, nc, f"ln_nm{i}", [128, 1], F32) for i in range(2)]
        ln_rt = [sb(st, nc, f"ln_rt{i}", [128, 1], F32) for i in range(2)]
        ln_rs = [sb(st, nc, f"ln_rs{i}", [128, 1], F32) for i in range(2)]
        sqs = sb(st, nc, "sqs", [128, D], BF16)
        xnT = [sb(st, nc, f"xnT{i}", [128, 1024], BF16) for i in range(8)]
        stage = [sb(st, nc, f"stage{i}", [128, 1408], F32) for i in range(2)]
        olb = [sb(st, nc, f"olb{i}", [128, 1024], BF16) for i in range(8)]
        wob = [sb(st, nc, f"wob{i}", [128, 1024], BF16) for i in range(8)]
        bob = sb(st, nc, "bob", [1, 1024], BF16)
        wgb = [sb(st, nc, f"wgb{i}", [128, 8], BF16) for i in range(8)]
        rbb = sb(st, nc, "rbb", [1, 8], BF16)
        w1b = [sb(st, nc, f"w1b{i}", [128, 1408], BF16) for i in range(8)]
        w2b = [sb(st, nc, f"w2b{i}", [128, 1408], BF16) for i in range(8)]
        w3b = [sb(st, nc, f"w3b{i}", [128, 1024], BF16) for i in range(NHC)]
        gb = [sb(st, nc, f"gb{i}", [128, 1024], BF16) for i in range(NHC)]
        sil = [sb(st, nc, f"sil{i}", [128, 512], F32) for i in range(2)]
        ones_row = sb(st, nc, "ones_row", [1, 512], BF16)
        ones_tile = sb(st, nc, "ones_tile", [128, 128], BF16)
        iden = sb(st, nc, "iden", [128, 128], BF16)
        epsln = sb(st, nc, "epsln", [128, 1], F32)
        gate_s = sb(st, nc, "gate_s", [128, 8], F32)
        gm = sb(st, nc, "gm", [128, 1], F32)
        gnm = sb(st, nc, "gnm", [128, 1], F32)
        gm2 = sb(st, nc, "gm2", [128, 1], F32)
        gsel = sb(st, nc, "gsel", [128, 8], F32)
        gcur = sb(st, nc, "gcur", [128, 8], F32)
        gcur2 = sb(st, nc, "gcur2", [128, 8], F32)
        gex = sb(st, nc, "gex", [128, 8], F32)
        gexm = sb(st, nc, "gexm", [128, 8], F32)
        gden = sb(st, nc, "gden", [128, 1], F32)
        grd = sb(st, nc, "grd", [128, 1], F32)
        w_all = [sb(st, nc, f"w_all{i}", [128, 8], F32) for i in range(NTT)]
        wcol = sb(st, nc, "wcol", [128, 1], F32)

        ptrT = ps(st, nc, "ptrT", [128, 1024], BF16)
        pproj = [ps(st, nc, f"pproj{i}", [128, 512], F32) for i in range(2)]
        pgate = ps(st, nc, "pgate", [128, 8], F32)
        ph1 = ps(st, nc, "ph1", [128, 512], F32)
        ph2 = ps(st, nc, "ph2", [128, 512], F32)
        peo = [ps(st, nc, f"peo{i}", [128, 512], F32) for i in range(2)]

        sld = sem(st, nc, "sld")
        sv = sem(st, nc, "sv")
        ss = sem(st, nc, "ss")
        stn = sem(st, nc, "stn")
        sgp = sem(st, nc, "sgp")
        sst = sem(st, nc, "sst")
        svr = sem(st, nc, "svr")

        LOADS_FIXED = 8 + 8 + 1 + 8 + 8 + 1
        LOADS_EH = 27
        LOADS_PASS = LOADS_FIXED + NE * NHALF * LOADS_EH

        def mk_loads(p):
            ls = []
            tb = p * 1024
            for tt in range(NTT):
                ls.append((xr_d[tb + tt * 128: tb + (tt + 1) * 128, :], None, acc[tt], D))
            for dc in range(8):
                ls.append((wout_d[dc * 128:(dc + 1) * 128, :], 'st', wob[dc], 1024))
            ls.append((bout_d, 'row', bob, 1024))
            for dc in range(8):
                ls.append((olar_d[dc * 128:(dc + 1) * 128, tb:tb + 1024], 'st', olb[dc], 1024))
            for dc in range(8):
                ls.append((wg_d[dc * 128:(dc + 1) * 128, :], 'st', wgb[dc], 8))
            ls.append((rb_d, 'row', rbb, 8))
            for e in range(NE):
                for hf in range(NHALF):
                    for dc in range(8):
                        ls.append((w1_d[e, dc, :, hf * 1408:(hf + 1) * 1408], 'st', w1b[dc], 1408))
                    for dc in range(8):
                        ls.append((w2_d[e, dc, :, hf * 1408:(hf + 1) * 1408], 'st', w2b[dc], 1408))
                    for hc in range(NHC):
                        ls.append((w3_d[e, hf * NHC + hc, :, :], 'st', w3b[hc], 1024))
            assert len(ls) == LOADS_PASS
            return ls

        def ld_done(p, i):
            return 16 * (LOADS_PASS * p + i + 1)

        def cast_done(p, i):
            return 2 + LOADS_PASS * p + i + 1

        IX_X0 = 0
        IX_WOUT = 8
        IX_BOUT = 16
        IX_OLAR = 17
        IX_WG = 25
        IX_RB = 33
        def ix_eh(e, hf):   return LOADS_FIXED + (e * NHALF + hf) * LOADS_EH

        V_EH = NHC * 2 + 16
        V_PASS = 24 + 16 + 16 + NE * NHALF * V_EH

        def v_xc(p, tt):   return 1 + V_PASS * p + 3 * tt + 1
        def v_xnb(p, tt):  return 1 + V_PASS * p + 3 * tt + 2
        def v_xnT(p, tt):  return 1 + V_PASS * p + 3 * tt + 3
        def v_ln_all(p):   return 1 + V_PASS * p + 24
        def v_b2(p, k):    return v_ln_all(p) + k + 1
        def v_gpre(p, tt): return v_ln_all(p) + 16 + 2 * tt + 1
        def v_gate(p, tt): return v_ln_all(p) + 16 + 2 * tt + 2
        def v_gate_all(p): return v_ln_all(p) + 16 + 16
        def v_g(p, e, hf, j):
            return v_gate_all(p) + (e * NHALF + hf) * V_EH + j + 1
        def v_eo(p, e, hf, k):
            return v_gate_all(p) + (e * NHALF + hf) * V_EH + 22 + k + 1

        S_EH = NHC * 2
        S_PASS = 8 + 8 + NE * NHALF * S_EH
        def s_ln(p, tt):   return S_PASS * p + tt + 1
        def s_gate(p, tt): return S_PASS * p + 8 + tt + 1
        def s_sil(p, e, hf, j): return S_PASS * p + 16 + (e * NHALF + hf) * S_EH + j + 1

        T_EH = NHC * 2 + 16
        T_PASS = 8 + 16 + 8 + NE * NHALF * T_EH
        def t_tr(p, tt):   return T_PASS * p + tt + 1
        def t_b2(p, k):    return T_PASS * p + 8 + k + 1
        def t_gate(p, tt): return T_PASS * p + 24 + tt + 1
        def t_g(p, e, hf, j):  return T_PASS * p + 32 + (e * NHALF + hf) * T_EH + j + 1
        def t_eo(p, e, hf, k): return T_PASS * p + 32 + (e * NHALF + hf) * T_EH + 22 + k + 1

        @block_holder(nc)
        def build(block):
            @block.sync
            def _(sync):
                for p in range(NPASS):
                    ls = mk_loads(p)
                    if p == 1:
                        sync.wait_ge(sst, 16 * 8)
                    for i, (src, kind, dst, w) in enumerate(ls):
                        gi = LOADS_PASS * p + i
                        if kind is None:
                            sync.dma_start(out=dst[:, :], in_=src).then_inc(sld, 16)
                        else:
                            if gi >= 2:
                                g2 = gi - 2
                                sync.wait_ge(sgp, 2 + g2 + 1)
                            sdst = stage[gi % 2][0:1, 0:w] if kind == 'row' else stage[gi % 2][:, 0:w]
                            sync.dma_start(out=sdst, in_=src).then_inc(sld, 16)

            @block.gpsimd
            def _(g):
                g.memset(ones_row[:], 1.0)
                g.memset(ones_tile[:], 1.0).then_inc(sgp, 1)
                g.affine_select(iden[:], ones_tile[:], [[1, 128]], ALU.is_equal, 0.0,
                                base=0, channel_multiplier=-1).then_inc(sgp, 1)
                for p in range(NPASS):
                    ls = mk_loads(p)
                    for i, (src, kind, dst, w) in enumerate(ls):
                        gi = LOADS_PASS * p + i
                        if i == IX_WOUT and p == 1:
                            g.wait_ge(stn, t_gate(0, NTT - 1))
                        for e in range(NE):
                            for hf in range(NHALF):
                                if i == ix_eh(e, hf):
                                    if e + hf > 0:
                                        pe, phf = (e, 0) if hf == 1 else (e - 1, 1)
                                        g.wait_ge(stn, t_eo(p, pe, phf, 15))
                                    elif p == 1:
                                        g.wait_ge(stn, t_eo(0, NE - 1, NHALF - 1, 15))
                        g.wait_ge(sld, 16 * (gi + 1))
                        if kind is None:
                            g.engine_nop().then_inc(sgp, 1)
                        elif kind == 'row':
                            g.tensor_copy(dst[0:1, 0:w], stage[gi % 2][0:1, 0:w]).then_inc(sgp, 1)
                        else:
                            g.tensor_copy(dst[:, 0:w], stage[gi % 2][:, 0:w]).then_inc(sgp, 1)

            @block.vector
            def _(v):
                nvr = [0]

                def vguard(ins):
                    ins.then_inc(svr, 1)
                    nvr[0] += 1
                    v.wait_ge(svr, nvr[0])

                v.memset(epsln[:], LN_EPS).then_inc(sv, 1)
                for p in range(NPASS):
                    for tt in range(NTT):
                        bi = tt % 2
                        v.wait_ge(sgp, cast_done(p, IX_X0 + tt))
                        vguard(v.tensor_reduce(ln_sum[bi][:, :], acc[tt][:, :], axis=AX.X, op=ALU.add))
                        vguard(v.tensor_scalar(ln_nm[bi][:, :], ln_sum[bi][:, :], -1.0 / D, None, ALU.mult))
                        v.tensor_scalar(xnf[bi][:, :], acc[tt][:, :], ln_nm[bi][:, :], None, ALU.add).then_inc(sv, 1)
                        v.wait_ge(ss, s_ln(p, tt))
                        vguard(v.reciprocal(ln_rs[bi][:, :], ln_rt[bi][:, :]))
                        v.tensor_scalar(xnb[bi][:, :], xnf[bi][:, :], ln_rs[bi][:, :], None, ALU.mult).then_inc(sv, 1)
                        v.wait_ge(stn, t_tr(p, tt))
                        for dc in range(8):
                            ins = v.tensor_copy(xnT[dc][:, tt * 128:(tt + 1) * 128],
                                                ptrT[:, dc * 128:(dc + 1) * 128])
                        ins.then_inc(sv, 1)
                    for k in range(16):
                        tt, dh = k // 2, k % 2
                        v.wait_ge(stn, t_b2(p, k))
                        v.scalar_tensor_tensor(acc[tt][:, dh * 512:(dh + 1) * 512],
                                               pproj[k % 2][:, :], 1.0,
                                               acc[tt][:, dh * 512:(dh + 1) * 512],
                                               ALU.mult, ALU.add).then_inc(sv, 1)
                    for tt in range(NTT):
                        v.wait_ge(stn, t_gate(p, tt))
                        vguard(v.tensor_copy(gate_s[:, :], pgate[:, :]))
                        vguard(v.tensor_reduce(gm[:, :], gate_s[:, :], axis=AX.X, op=ALU.max))
                        vguard(v.tensor_scalar(gnm[:, :], gm[:, :], -1.0, None, ALU.mult))
                        vguard(v.tensor_scalar(gsel[:, :], gate_s[:, :], gm[:, :], None, ALU.is_ge))
                        vguard(v.scalar_tensor_tensor(gcur[:, :], gsel[:, :], -1e30, gate_s[:, :],
                                                      ALU.mult, ALU.add))
                        vguard(v.tensor_reduce(gm2[:, :], gcur[:, :], axis=AX.X, op=ALU.max))
                        vguard(v.tensor_scalar(gsel[:, :], gcur[:, :], gm2[:, :], None, ALU.is_ge))
                        vguard(v.scalar_tensor_tensor(gcur2[:, :], gsel[:, :], -1e30, gcur[:, :],
                                                      ALU.mult, ALU.add))
                        vguard(v.tensor_reduce(gm2[:, :], gcur2[:, :], axis=AX.X, op=ALU.max))
                        vguard(v.tensor_scalar(gsel[:, :], gcur2[:, :], gm2[:, :], None, ALU.is_ge))
                        vguard(v.scalar_tensor_tensor(gcur[:, :], gsel[:, :], -1e30, gcur2[:, :],
                                                      ALU.mult, ALU.add))
                        vguard(v.tensor_reduce(gm2[:, :], gcur[:, :], axis=AX.X, op=ALU.max))
                        vguard(v.tensor_scalar(gsel[:, :], gate_s[:, :], gm2[:, :], None, ALU.is_ge))
                        v.engine_nop().then_inc(sv, 1)
                        v.wait_ge(ss, s_gate(p, tt))
                        vguard(v.tensor_tensor(gexm[:, :], gex[:, :], gsel[:, :], ALU.mult))
                        vguard(v.tensor_reduce(gden[:, :], gexm[:, :], axis=AX.X, op=ALU.add))
                        vguard(v.reciprocal(grd[:, :], gden[:, :]))
                        v.tensor_scalar(w_all[tt][:, :], gexm[:, :], grd[:, :], None, ALU.mult).then_inc(sv, 1)
                    for e in range(NE):
                        for hf in range(NHALF):
                            for j in range(NHC * 2):
                                hc, strip = j // 2, j % 2
                                v.wait_ge(stn, t_g(p, e, hf, j))
                                v.wait_ge(ss, s_sil(p, e, hf, j))
                                v.tensor_tensor(gb[hc][:, strip * 512:(strip + 1) * 512],
                                                sil[j % 2][:, :], ph2[:, :], ALU.mult).then_inc(sv, 1)
                            for k in range(16):
                                tsub, dh = k // 2, k % 2
                                v.wait_ge(stn, t_eo(p, e, hf, k))
                                asl = acc[tsub][:, dh * 512:(dh + 1) * 512]
                                if e < 8:
                                    v.scalar_tensor_tensor(asl, peo[k % 2][:, :],
                                                           w_all[tsub][:, e:e + 1], asl,
                                                           ALU.mult, ALU.add).then_inc(sv, 1)
                                else:
                                    v.scalar_tensor_tensor(asl, peo[k % 2][:, :], 1.0, asl,
                                                           ALU.mult, ALU.add).then_inc(sv, 1)

            @block.scalar
            def _(s):
                for p in range(NPASS):
                    for tt in range(NTT):
                        bi = tt % 2
                        s.wait_ge(sv, v_xc(p, tt))
                        s.activation(sqs[:, :], xnf[bi][:, :], AF.Square, accum_out=ln_sum[bi][:, :])
                        s.activation(ln_rt[bi][:, :], ln_sum[bi][:, :], AF.Sqrt,
                                     scale=1.0 / D, bias=epsln[:, :]).then_inc(ss, 1)
                    for tt in range(NTT):
                        s.wait_ge(sv, v_gpre(p, tt))
                        s.activation(gex[:, :], gate_s[:, :], AF.Exp, bias=gnm[:, :]).then_inc(ss, 1)
                    for e in range(NE):
                        for hf in range(NHALF):
                            for j in range(NHC * 2):
                                s.wait_ge(stn, t_g(p, e, hf, j))
                                s.activation(sil[j % 2][:, :], ph1[:, :], AF.Silu).then_inc(ss, 1)
                    for tt in range(NTT):
                        s.wait_ge(sv, v_eo(p, NE - 1, NHALF - 1, 15))
                        s.dma_start(out=x1_d[p * 1024 + tt * 128:p * 1024 + (tt + 1) * 128, :],
                                    in_=acc[tt][:, :]).then_inc(sst, 16)
                s.wait_ge(sst, 16 * 16)

            @block.tensor
            def _(t):
                t.wait_ge(sgp, 2)
                for p in range(NPASS):
                    for tt in range(NTT):
                        t.wait_ge(sv, v_xnb(p, tt))
                        gi = p * NTT + tt
                        if gi >= 1:
                            g2 = gi - 1
                            t.wait_ge(sv, v_xnT(g2 // NTT, g2 % NTT))
                        for dc in range(8):
                            ins = t.transpose(ptrT[:, dc * 128:(dc + 1) * 128],
                                              xnb[tt % 2][:, dc * 128:(dc + 1) * 128], iden[:, :])
                        ins.then_inc(stn, 1)
                    t.wait_ge(sgp, cast_done(p, IX_OLAR + 7))
                    for k in range(16):
                        tt, dh = k // 2, k % 2
                        if k == 0:
                            t.wait_ge(sv, v_ln_all(p))
                        if k >= 2:
                            t.wait_ge(sv, v_b2(p, k - 2))
                        elif p == 1:
                            t.wait_ge(sv, v_b2(0, 14 + k))
                        for dc in range(8):
                            t.matmul(pproj[k % 2][:, :], olb[dc][:, tt * 128:(tt + 1) * 128],
                                     wob[dc][:, dh * 512:(dh + 1) * 512],
                                     start=(dc == 0), stop=False)
                        t.matmul(pproj[k % 2][:, :], ones_row[0:1, 0:128],
                                 bob[0:1, dh * 512:(dh + 1) * 512],
                                 start=False, stop=True).then_inc(stn, 1)
                    t.wait_ge(sgp, cast_done(p, IX_RB))
                    for tt in range(NTT):
                        if tt >= 1:
                            t.wait_ge(sv, v_gpre(p, tt - 1))
                        elif p == 1:
                            t.wait_ge(sv, v_gpre(0, 7))
                        for dc in range(8):
                            t.matmul(pgate[:, :], xnT[dc][:, tt * 128:(tt + 1) * 128],
                                     wgb[dc][:, :], start=(dc == 0), stop=False)
                        t.matmul(pgate[:, :], ones_row[0:1, 0:128], rbb[0:1, :],
                                 start=False, stop=True).then_inc(stn, 1)
                    for e in range(NE):
                        for hf in range(NHALF):
                            t.wait_ge(sgp, cast_done(p, ix_eh(e, hf) + LOADS_EH - 1))
                            for j in range(NHC * 2):
                                hc, strip = j // 2, j % 2
                                if j >= 1:
                                    t.wait_ge(sv, v_g(p, e, hf, j - 1))
                                elif e + hf > 0:
                                    pe, phf = (e, 0) if hf == 1 else (e - 1, 1)
                                    t.wait_ge(sv, v_g(p, pe, phf, NHC * 2 - 1))
                                elif p == 1:
                                    t.wait_ge(sv, v_g(0, NE - 1, NHALF - 1, NHC * 2 - 1))
                                for dc in range(8):
                                    t.matmul(ph1[:, :], w1b[dc][:, hc * 128:(hc + 1) * 128],
                                             xnT[dc][:, strip * 512:(strip + 1) * 512],
                                             start=(dc == 0), stop=(dc == 7))
                                for dc in range(8):
                                    ins = t.matmul(ph2[:, :], w2b[dc][:, hc * 128:(hc + 1) * 128],
                                                   xnT[dc][:, strip * 512:(strip + 1) * 512],
                                                   start=(dc == 0), stop=(dc == 7))
                                ins.then_inc(stn, 1)
                            for k in range(16):
                                tsub, dh = k // 2, k % 2
                                if k == 0:
                                    t.wait_ge(sv, v_g(p, e, hf, NHC * 2 - 1))
                                if k >= 2:
                                    t.wait_ge(sv, v_eo(p, e, hf, k - 2))
                                elif e + hf > 0:
                                    pe, phf = (e, 0) if hf == 1 else (e - 1, 1)
                                    t.wait_ge(sv, v_eo(p, pe, phf, 14 + k))
                                elif p == 1:
                                    t.wait_ge(sv, v_eo(0, NE - 1, NHALF - 1, 14 + k))
                                if k == 0 and e == 0 and hf == 0:
                                    t.wait_ge(sv, v_gate_all(p))
                                for hc in range(NHC):
                                    ins = t.matmul(peo[k % 2][:, :],
                                                   gb[hc][:, tsub * 128:(tsub + 1) * 128],
                                                   w3b[hc][:, dh * 512:(dh + 1) * 512],
                                                   start=(hc == 0), stop=(hc == NHC - 1))
                                ins.then_inc(stn, 1)
    return nc


def run_B(x_rows, olar_full, W_out, b_out, Wg, rbias, W1, W2, W3, Ws1, Ws2, Ws3):
    nc = build_B()

    def padh(a):
        return np.concatenate([a, np.zeros((2816 - a.shape[0], a.shape[1]), a.dtype)], axis=0)

    w1s, w2s, w3s = [], [], []
    for e in range(8):
        w1s.append(padh(W1[e]).T.reshape(8, 128, 2816))
        w2s.append(padh(W2[e]).T.reshape(8, 128, 2816))
        w3s.append(padh(W3[e].T).reshape(22, 128, 1024))
    w1s.append(padh(Ws1).T.reshape(8, 128, 2816))
    w2s.append(padh(Ws2).T.reshape(8, 128, 2816))
    w3s.append(padh(Ws3.T).reshape(22, 128, 1024))
    w1a = np.ascontiguousarray(np.stack(w1s))
    w2a = np.ascontiguousarray(np.stack(w2s))
    w3a = np.ascontiguousarray(np.stack(w3s))
    wga = np.ascontiguousarray(Wg.T)
    in_maps = []
    for c in range(8):
        b, hf = c // 2, c % 2
        in_maps.append({
            "xr": np.ascontiguousarray(x_rows[c]),
            "olar": np.ascontiguousarray(olar_full[b][:, hf * 2048:(hf + 1) * 2048]),
            "wout": W_out, "bout": b_out.reshape(1, D),
            "wg": wga, "rb": rbias.reshape(1, 8),
            "w1": w1a, "w2": w2a, "w3": w3a,
        })
    res = run_bass_kernel_spmd(nc, in_maps, core_ids=list(range(8)))
    return [res.results[c]["x1"] for c in range(8)]


def build_C():
    nc = bass.Bass("TRN2", target_bir_lowering=False, debug=False)
    xb = nc.dram_tensor("xb", [L, D], F32, kind="ExternalInput").ap()
    wqkv_d = nc.dram_tensor("wqkv", [D, 1536], F32, kind="ExternalInput").ap()
    delta_d = nc.dram_tensor("delta", [8, 64], F32, kind="ExternalInput").ap()
    o1_d = nc.dram_tensor("o1", [512, L], F32, kind="ExternalOutput").ap()

    NSUB, NTT = 4, 8
    NW = 16 * 9
    KCS = [4 * (qs + 1) for qs in range(8)]
    NKC = sum(KCS)
    st = ExitStack()
    with st:
        ln_sum = [sb(st, nc, f"ln_sum{i}", [128, 1], F32) for i in range(2)]
        ln_nm = [sb(st, nc, f"ln_nm{i}", [128, 1], F32) for i in range(2)]
        ln_rt = [sb(st, nc, f"ln_rt{i}", [128, 1], F32) for i in range(2)]
        ln_rs = [sb(st, nc, f"ln_rs{i}", [128, 1], F32) for i in range(2)]
        muq = [sb(st, nc, f"muq{i}", [128, L], BF16) for i in range(4)]
        muk = [sb(st, nc, f"muk{i}", [128, L], BF16) for i in range(4)]
        vt = [sb(st, nc, f"vt{i}", [128, 512], BF16) for i in range(32)]
        spa = [sb(st, nc, f"spa{i}", [128, 512], BF16) for i in range(2)]
        spr = [sb(st, nc, f"spr{i}", [128, 512], BF16) for i in range(2)]
        spe = [sb(st, nc, f"spe{i}", [128, 512], F32) for i in range(2)]
        spl = [sb(st, nc, f"spl{i}", [128, 512], F32) for i in range(2)]
        onesf = sb(st, nc, "onesf", [128, 1], F32)
        halfpi = sb(st, nc, "halfpi", [128, 1], F32)
        ones_row = sb(st, nc, "ones_row", [1, 512], BF16)
        ones_col = sb(st, nc, "ones_col", [128, 1], BF16)
        ones_tile = sb(st, nc, "ones_tile", [128, 128], BF16)
        iden = sb(st, nc, "iden", [128, 128], BF16)
        onestf = sb(st, nc, "onestf", [64, 64], F32)
        idenf = sb(st, nc, "idenf", [64, 64], F32)
        epsln = sb(st, nc, "epsln", [128, 1], F32)
        invf = sb(st, nc, "invf", [1, 64], F32)
        invf2 = sb(st, nc, "invf2", [1, 64], F32)
        thq = sb(st, nc, "thq", [64, 512], F32)
        thqi = sb(st, nc, "thqi", [64, 512], mybir.dt.int32)
        thqf = sb(st, nc, "thqf", [64, 512], F32)
        thr = sb(st, nc, "thr", [64, 512], F32)
        thm = sb(st, nc, "thm", [64, 512], F32)
        dlt = sb(st, nc, "dlt", [8, 64], F32)
        dl2 = sb(st, nc, "dl2", [8, 64], F32)
        sd2 = sb(st, nc, "sd2", [8, 64], F32)
        cd2 = sb(st, nc, "cd2", [8, 64], F32)
        cdd = sb(st, nc, "cdd", [8, 64], F32)
        sdd = sb(st, nc, "sdd", [8, 64], F32)
        cdc = sb(st, nc, "cdc", [64, 8], F32)
        sdc = sb(st, nc, "sdc", [64, 8], F32)
        nsdc = sb(st, nc, "nsdc", [64, 8], F32)
        nmk = sb(st, nc, "nmk", [128, 1], F32)
        nmb = sb(st, nc, "nmb", [128, 1], BF16)
        negU = sb(st, nc, "negU", [1, 512], BF16)
        Pt = sb(st, nc, "Pt", [128, 512], BF16)
        Ptm = sb(st, nc, "Ptm", [128, 512], BF16)
        dsb = sb(st, nc, "dsb", [1, 512], F32)
        rcd = sb(st, nc, "rcd", [1, 512], F32)
        dnb = sb(st, nc, "dnb", [1, 512], BF16)
        onum = sb(st, nc, "onum", [64, 512], F32)
        ostr = sb(st, nc, "ostr", [64, 512], F32)

        st2 = ExitStack()
        xt = [sb(st2, nc, f"xt{i}", [128, D], F32) for i in range(2)]
        xnf = [sb(st2, nc, f"xnf{i}", [128, D], F32) for i in range(2)]
        xnb = [sb(st2, nc, f"xnb{i}", [128, D], BF16) for i in range(2)]
        sqs2 = sb(st2, nc, "sqs", [128, D], BF16)
        xnT = [sb(st2, nc, f"xnT{i}", [128, 1024], BF16) for i in range(8)]
        wqs = sb(st2, nc, "wqs", [128, 1536], F32)
        wqb = [sb(st2, nc, f"wqb{i}", [128, 1536], BF16) for i in range(8)]
        st2.close()
        trow2 = sb(st, nc, "trow2", [1, 512], F32)
        mq0 = sb(st, nc, "mq0", [64, L], BF16)
        mk0 = sb(st, nc, "mk0", [64, L], BF16)
        qr = sb(st, nc, "qr", [64, L], BF16)
        qi = sb(st, nc, "qi", [64, L], BF16)
        kr = sb(st, nc, "kr", [64, L], BF16)
        ki = sb(st, nc, "ki", [64, L], BF16)
        cosF = sb(st, nc, "cosF", [64, L], BF16)
        sinF = sb(st, nc, "sinF", [64, L], BF16)

        ptrT = ps(st, nc, "ptrT", [128, 1024], BF16)
        pq = [ps(st, nc, f"pq{i}", [128, 512], F32) for i in range(2)]
        pS = ps(st, nc, "pS", [128, 512], F32)
        pO = ps(st, nc, "pO", [64, 512], F32)
        pd1 = ps(st, nc, "pd1", [1, 512], F32)
        pU = ps(st, nc, "pU", [64, 512], F32)
        pbc = ps(st, nc, "pbc", [64, 512], F32)

        sld = sem(st, nc, "sld")
        sv = sem(st, nc, "sv")
        ss = sem(st, nc, "ss")
        stn = sem(st, nc, "stn")
        sgp = sem(st, nc, "sgp")
        sst = sem(st, nc, "sst")
        svr = sem(st, nc, "svr")
        ssr = sem(st, nc, "ssr")
        sgr = sem(st, nc, "sgr")
        sldx = [sem(st, nc, f"sldx{i}") for i in range(2)]
        sldm = sem(st, nc, "sldm")

        V_SUB = 24 + 32 + 8

        def v_xc(su, tt):   return 1 + V_SUB * su + 3 * tt + 1
        def v_xnb(su, tt):  return 1 + V_SUB * su + 3 * tt + 2
        def v_xnT(su, tt):  return 1 + V_SUB * su + 3 * tt + 3
        def v_ln_all(su):   return 1 + V_SUB * su + 24
        def v_qk(su, k):    return v_ln_all(su) + 2 * (k + 1)
        def v_vt(su, i):    return v_ln_all(su) + 32 + i + 1
        V_C2 = 1 + NSUB * V_SUB
        def v_rope(n):      return V_C2 + n
        V_ATT0 = V_C2 + 18
        def v_build(h):     return V_ATT0 + 25 * h + 1
        def v_negu(h, qs):  return V_ATT0 + 25 * h + 1 + 3 * qs + 1
        def v_den(h, qs):   return V_ATT0 + 25 * h + 1 + 3 * qs + 2
        def v_outs(h, qs):  return V_ATT0 + 25 * h + 1 + 3 * qs + 3

        S_SUB = 8 + 32

        def s_ln(su, tt):   return S_SUB * su + tt + 1
        def s_qk(su, k):    return S_SUB * su + 8 + 2 * (k + 1)
        S_C2 = NSUB * S_SUB
        def s_rope(n):      return S_C2 + n
        S_ATT0 = S_C2 + 18

        def s_exp(h, qs, kc):
            return S_ATT0 + NKC * h + sum(KCS[:qs]) + kc + 1

        T_SUB = 32

        def t_tr(su, tt):   return T_SUB * su + tt + 1
        def t_qk(su, k):    return T_SUB * su + 8 + k + 1
        def t_vt(su, i):    return T_SUB * su + 24 + i + 1
        T_C2 = NSUB * T_SUB
        def t_rope(n):      return T_C2 + n
        T_ATT0 = T_C2 + 9
        T_H = 2 * NKC + 2 * 8

        def t_base(h, qs):  return T_ATT0 + T_H * h + sum(2 * KCS[q] + 2 for q in range(qs))
        def t_negu(h, qs):  return t_base(h, qs) + 1
        def t_ps(h, qs, kc):  return t_base(h, qs) + 1 + 2 * kc + 1
        def t_pod(h, qs, kc): return t_base(h, qs) + 1 + 2 * kc + 2
        def t_pbc(h, qs):   return t_base(h, qs) + 1 + 2 * KCS[qs] + 1

        def g_wc(i):        return 3 + i + 1
        def g_iota(strip):
            return 11 + strip + 1
        def g_mask(h, qs, m):
            return 19 + 32 * h + 4 * qs + m + 1

        @block_holder(nc)
        def build(block):
            @block.sync
            def _(sync):
                sync.dma_start(out=dlt[:, :], in_=delta_d).then_inc(sld, 16)
                for dc in range(8):
                    sync.wait_ge(sgp, g_wc(dc - 1) if dc > 0 else 3)
                    sync.dma_start(out=wqs[:, :], in_=wqkv_d[dc * 128:(dc + 1) * 128, :]).then_inc(sld, 16)
                for su in range(NSUB):
                    for tt in range(NTT):
                        gi = su * NTT + tt
                        if gi >= 2:
                            g2 = gi - 2
                            sync.wait_ge(sv, v_xnb(g2 // NTT, g2 % NTT))
                        sync.dma_start(out=xt[gi % 2][:, :],
                                       in_=xb[gi * 128:(gi + 1) * 128, :]).then_inc(sldx[gi % 2], 16)
                for h in range(8):
                    c, off = h // 2, (h % 2) * 64
                    sl = slice(off, off + 64)
                    sync.wait_ge(stn, t_pbc(h - 1, 7) if h > 0 else T_C2)
                    sync.dma_start(out=mq0[:, :], in_=muq[c][sl, :]).then_inc(sldm, 16)
                    sync.dma_start(out=mk0[:, :], in_=muk[c][sl, :]).then_inc(sldm, 16)

            @block.gpsimd
            def _(g):
                ngr = [0]

                def gguard(ins):
                    ins.then_inc(sgr, 1)
                    ngr[0] += 1
                    g.wait_ge(sgr, ngr[0])

                g.memset(ones_row[:], 1.0)
                g.memset(ones_col[:], 1.0)
                g.memset(onesf[:], 1.0)
                g.memset(halfpi[:], float(np.pi / 2))
                g.memset(ones_tile[:], 1.0)
                gguard(g.memset(onestf[:], 1.0))
                g.nop().then_inc(sgp, 1)
                g.affine_select(iden[:], ones_tile[:], [[1, 128]], ALU.is_equal, 0.0,
                                base=0, channel_multiplier=-1).then_inc(sgp, 1)
                g.affine_select(idenf[:], onestf[:], [[1, 64]], ALU.is_equal, 0.0,
                                base=0, channel_multiplier=-1)
                g.iota(invf[:, :], [[1, 64]], base=0, channel_multiplier=0,
                       allow_small_or_imprecise_dtypes=True).then_inc(sgp, 1)
                for dc in range(8):
                    g.wait_ge(sld, 16 + 16 * (dc + 1))
                    g.tensor_copy(wqb[dc][:, :], wqs[:, :]).then_inc(sgp, 1)
                for strip in range(8):
                    g.wait_ge(stn, T_C2 if strip == 0 else t_rope(strip))
                    g.iota(trow2[:, :], [[1, 512]], base=strip * 512, channel_multiplier=0,
                           allow_small_or_imprecise_dtypes=True).then_inc(sgp, 1)
                prev_diag = None
                for h in range(8):
                    for qs in range(8):
                        for m in range(4):
                            kc = qs * 4 + m
                            g.wait_ge(ss, s_exp(h, qs, kc))
                            if prev_diag is not None:
                                g.wait_ge(stn, t_pod(*prev_diag))
                            g.affine_select(Ptm[:, :], Pt[:, :], [[1, 512]], ALU.is_ge, 0.0,
                                            base=qs * 512 - kc * 128,
                                            channel_multiplier=-1).then_inc(sgp, 1)
                            prev_diag = (h, qs, kc)

            @block.vector
            def _(v):
                nvr = [0]

                def vguard(ins):
                    ins.then_inc(svr, 1)
                    nvr[0] += 1
                    v.wait_ge(svr, nvr[0])

                v.memset(epsln[:], LN_EPS).then_inc(sv, 1)
                for su in range(NSUB):
                    for tt in range(NTT):
                        gi = su * NTT + tt
                        bi = gi % 2
                        v.wait_ge(sldx[gi % 2], 16 * (gi // 2 + 1))
                        vguard(v.tensor_reduce(ln_sum[bi][:, :], xt[bi][:, :], axis=AX.X, op=ALU.add))
                        vguard(v.tensor_scalar(ln_nm[bi][:, :], ln_sum[bi][:, :], -1.0 / D, None, ALU.mult))
                        v.tensor_scalar(xnf[bi][:, :], xt[bi][:, :], ln_nm[bi][:, :], None, ALU.add).then_inc(sv, 1)
                        v.wait_ge(ss, s_ln(su, tt))
                        vguard(v.reciprocal(ln_rs[bi][:, :], ln_rt[bi][:, :]))
                        v.tensor_scalar(xnb[bi][:, :], xnf[bi][:, :], ln_rs[bi][:, :], None, ALU.mult).then_inc(sv, 1)
                        v.wait_ge(stn, t_tr(su, tt))
                        for dc in range(8):
                            ins = v.tensor_copy(xnT[dc][:, tt * 128:(tt + 1) * 128],
                                                ptrT[:, dc * 128:(dc + 1) * 128])
                        ins.then_inc(sv, 1)
                    for k in range(16):
                        oc, strip = k // 2, k % 2
                        bi = k % 2
                        col0 = su * 1024 + strip * 512
                        v.wait_ge(stn, t_qk(su, k))
                        v.tensor_scalar(spa[bi][:, :], pq[bi][:, :], 0.0, None, ALU.min)
                        v.tensor_scalar(spr[bi][:, :], pq[bi][:, :], 0.0, None, ALU.max)
                        v.tensor_tensor(spa[bi][:, :], spa[bi][:, :], spr[bi][:, :],
                                        ALU.subtract).then_inc(sv, 1)
                        v.wait_ge(ss, s_qk(su, k))
                        dst = (muq if oc < 4 else muk)[oc % 4]
                        v.tensor_tensor(dst[:, col0:col0 + 512], spr[bi][:, :], spl[bi][:, :],
                                        ALU.add).then_inc(sv, 1)
                    for i in range(8):
                        v.wait_ge(stn, t_vt(su, i))
                        v.tensor_copy(vt[su * 8 + i][:, :], pq[i % 2][:, :]).then_inc(sv, 1)
                for j in range(16):
                    strip, cos_path = j // 2, j % 2
                    v.wait_ge(stn, t_rope(strip + 1))
                    if j >= 1:
                        v.wait_ge(ss, s_rope(1 + j))
                    off = float(np.pi / 2) if cos_path else 0.0
                    vguard(v.tensor_scalar(thq[:, :], pU[:, :], 1.0 / (2 * np.pi),
                                           off / (2 * np.pi), ALU.mult, ALU.add))
                    vguard(v.tensor_copy(thqi[:, :], thq[:, :]))
                    vguard(v.tensor_copy(thqf[:, :], thqi[:, :]))
                    if off != 0.0:
                        vguard(v.tensor_scalar(thr[:, :], pU[:, :], off, None, ALU.add))
                        vguard(v.scalar_tensor_tensor(thr[:, :], thqf[:, :], -2.0 * np.pi, thr[:, :],
                                                      ALU.mult, ALU.add))
                    else:
                        vguard(v.scalar_tensor_tensor(thr[:, :], thqf[:, :], -2.0 * np.pi, pU[:, :],
                                                      ALU.mult, ALU.add))
                    vguard(v.tensor_scalar(thm[:, :], thr[:, :], float(np.pi), None, ALU.is_gt))
                    v.scalar_tensor_tensor(thr[:, :], thm[:, :], -2.0 * np.pi, thr[:, :],
                                           ALU.mult, ALU.add).then_inc(sv, 1)
                v.wait_ge(sld, 16)
                vguard(v.tensor_scalar(dl2[:, :], dlt[:, :], -2.0 * np.pi, 0.0, ALU.max, ALU.min))
                v.engine_nop().then_inc(sv, 1)
                v.wait_ge(ss, s_rope(18))
                vguard(v.tensor_tensor(cdd[:, :], sd2[:, :], sd2[:, :], ALU.mult))
                vguard(v.tensor_scalar(cdd[:, :], cdd[:, :], -2.0, 1.0, ALU.mult, ALU.add))
                vguard(v.tensor_tensor(sdd[:, :], sd2[:, :], cd2[:, :], ALU.mult))
                vguard(v.tensor_scalar(sdd[:, :], sdd[:, :], 2.0, None, ALU.mult))
                v.engine_nop().then_inc(sv, 1)
                v.wait_ge(stn, t_rope(9))
                vguard(v.tensor_copy(cdc[:, :], pU[0:64, 0:8]))
                vguard(v.tensor_copy(sdc[:, :], pU[0:64, 8:16]))
                vguard(v.tensor_scalar(nsdc[:, :], sdc[:, :], -1.0, None, ALU.mult))
                for h in range(8):
                    c, off = h // 2, (h % 2) * 64
                    sl = slice(off, off + 64)
                    if h == 0:
                        v.wait_ge(ss, s_rope(17))
                    v.wait_ge(sldm, 32 * (h + 1))
                    v.tensor_tensor(qr[:, :], mq0[:, :], cosF[:, :], ALU.mult)
                    v.tensor_tensor(qi[:, :], mq0[:, :], sinF[:, :], ALU.mult)
                    v.tensor_scalar(kr[:, :], cosF[:, :], cdc[:, h:h + 1], None, ALU.mult)
                    v.scalar_tensor_tensor(kr[:, :], sinF[:, :], nsdc[:, h:h + 1], kr[:, :],
                                           ALU.mult, ALU.add)
                    v.tensor_tensor(kr[:, :], kr[:, :], mk0[:, :], ALU.mult)
                    v.tensor_scalar(ki[:, :], sinF[:, :], cdc[:, h:h + 1], None, ALU.mult)
                    v.scalar_tensor_tensor(ki[:, :], cosF[:, :], sdc[:, h:h + 1], ki[:, :],
                                           ALU.mult, ALU.add)
                    v.tensor_tensor(ki[:, :], ki[:, :], mk0[:, :], ALU.mult)
                    vguard(v.tensor_reduce(nmk[0:64, :], mk0[:, :], axis=AX.X, op=ALU.max))
                    v.tensor_scalar(nmb[0:64, :], nmk[0:64, :], -1.0, None, ALU.mult).then_inc(sv, 1)
                    for qs in range(8):
                        n_store = h * 8 + qs
                        v.wait_ge(stn, t_negu(h, qs))
                        vguard(v.tensor_copy(negU[:, :], pU[0:1, :]))
                        v.engine_nop().then_inc(sv, 1)
                        v.wait_ge(stn, t_pod(h, qs, KCS[qs] - 1))
                        vguard(v.tensor_scalar(dsb[:, :], pd1[:, :], 0.0, None, ALU.add))
                        vguard(v.reciprocal(rcd[:, :], dsb[:, :]))
                        v.tensor_copy(dnb[:, :], rcd[:, :])
                        vguard(v.tensor_copy(onum[:, :], pO[:, :]))
                        v.engine_nop().then_inc(sv, 1)
                        v.wait_ge(stn, t_pbc(h, qs))
                        if n_store >= 1:
                            v.wait_ge(sst, 16 * n_store)
                        v.tensor_tensor(ostr[:, :], onum[:, :], pbc[:, :], ALU.mult).then_inc(sv, 1)

            @block.scalar
            def _(s):
                nsr = [0]

                def sguard(ins):
                    ins.then_inc(ssr, 1)
                    nsr[0] += 1
                    s.wait_ge(ssr, nsr[0])

                for su in range(NSUB):
                    for tt in range(NTT):
                        bi = (su * NTT + tt) % 2
                        s.wait_ge(sv, v_xc(su, tt))
                        s.activation(sqs2[:, :], xnf[bi][:, :], AF.Square, accum_out=ln_sum[bi][:, :])
                        s.activation(ln_rt[bi][:, :], ln_sum[bi][:, :], AF.Sqrt,
                                     scale=1.0 / D, bias=epsln[:, :]).then_inc(ss, 1)
                    for k in range(16):
                        bi = k % 2
                        s.wait_ge(sv, v_qk(su, k) - 1)
                        sguard(s.activation(spe[bi][:, :], spa[bi][:, :], AF.Exp, scale=1.0))
                        s.nop().then_inc(ss, 1)
                        s.activation(spl[bi][:, :], spe[bi][:, :], AF.Ln,
                                     bias=onesf[:, :]).then_inc(ss, 1)
                s.wait_ge(sgp, 3)
                sguard(s.activation(invf2[:, :], invf[:, :], AF.Exp,
                                    scale=-float(np.log(10000.0) / 64.0)))
                s.nop().then_inc(ss, 1)
                for j in range(16):
                    strip, cos_path = j // 2, j % 2
                    s.wait_ge(sv, v_rope(j + 1))
                    dstF = cosF if cos_path else sinF
                    s.activation(dstF[:, strip * 512:(strip + 1) * 512], thr[:, :],
                                 AF.Sin).then_inc(ss, 1)
                s.wait_ge(sv, v_rope(17))
                sguard(s.activation(sd2[:, :], dl2[:, :], AF.Sin, scale=0.5))
                s.activation(cd2[:, :], dl2[:, :], AF.Sin, scale=0.5,
                             bias=halfpi[0:8, :]).then_inc(ss, 1)
                prev_diag = None
                for h in range(8):
                    for qs in range(8):
                        for kc in range(KCS[qs]):
                            s.wait_ge(stn, t_ps(h, qs, kc))
                            if prev_diag is not None:
                                s.wait_ge(sgp, g_mask(*prev_diag))
                                prev_diag = None
                            s.activation(Pt[:, :], pS[:, :], AF.Exp, scale=0.125).then_inc(ss, 1)
                            if kc >= qs * 4:
                                prev_diag = (h, qs, kc - qs * 4)
                        s.wait_ge(sv, v_outs(h, qs))
                        s.dma_start(out=o1_d[h * 64:(h + 1) * 64, qs * 512:(qs + 1) * 512],
                                    in_=ostr[:, :]).then_inc(sst, 16)
                s.wait_ge(sst, 16 * 64)

            @block.tensor
            def _(t):
                t.wait_ge(sgp, g_wc(7))
                for su in range(NSUB):
                    for tt in range(NTT):
                        gi = su * NTT + tt
                        t.wait_ge(sv, v_xnb(su, tt))
                        if gi >= 1:
                            g2 = gi - 1
                            t.wait_ge(sv, v_xnT(g2 // NTT, g2 % NTT))
                        for dc in range(8):
                            ins = t.transpose(ptrT[:, dc * 128:(dc + 1) * 128],
                                              xnb[gi % 2][:, dc * 128:(dc + 1) * 128], iden[:, :])
                        ins.then_inc(stn, 1)
                    for k in range(16):
                        oc, strip = k // 2, k % 2
                        bi = k % 2
                        if k == 0:
                            t.wait_ge(sv, v_ln_all(su))
                        if k >= 2:
                            t.wait_ge(sv, v_qk(su, k - 2))
                        elif su > 0:
                            t.wait_ge(sv, v_vt(su - 1, 6 + k))
                        for dc in range(8):
                            ins = t.matmul(pq[bi][:, :], wqb[dc][:, oc * 128:(oc + 1) * 128],
                                           xnT[dc][:, strip * 512:(strip + 1) * 512],
                                           start=(dc == 0), stop=(dc == 7))
                        ins.then_inc(stn, 1)
                    for i in range(8):
                        bi = i % 2
                        if i >= 2:
                            t.wait_ge(sv, v_vt(su, i - 2))
                        else:
                            t.wait_ge(sv, v_qk(su, 14 + i))
                        for dc in range(8):
                            ins = t.matmul(pq[bi][:, :], xnT[dc][:, i * 128:(i + 1) * 128],
                                           wqb[dc][:, 1024:1536], start=(dc == 0), stop=(dc == 7))
                        ins.then_inc(stn, 1)
                t.wait_ge(ss, s_rope(1))
                for strip in range(8):
                    t.wait_ge(sgp, g_iota(strip))
                    if strip >= 1:
                        t.wait_ge(sv, v_rope(2 * strip))
                    t.matmul(pU[0:64, :], invf2[:, :], trow2[0:1, :],
                             start=True, stop=True).then_inc(stn, 1)
                t.wait_ge(sv, v_rope(16))
                t.wait_ge(sv, v_rope(18))
                t.transpose(pU[0:64, 0:8], cdd[:, :], idenf[0:8, 0:8])
                t.transpose(pU[0:64, 8:16], sdd[:, :], idenf[0:8, 0:8]).then_inc(stn, 1)
                for h in range(8):
                    c, off = h // 2, (h % 2) * 64
                    sl = slice(off, off + 64)
                    for qs in range(8):
                        t.wait_ge(sv, v_build(h))
                        if not (h == 0 and qs == 0):
                            t.wait_ge(sv, v_outs(h - 1, 7) if qs == 0 else v_outs(h, qs - 1))
                        t.matmul(pU[0:1, :], nmb[0:64, :], mq0[:, qs * 512:(qs + 1) * 512],
                                 start=True, stop=True).then_inc(stn, 1)
                        t.wait_ge(sv, v_negu(h, qs))
                        for kc in range(KCS[qs]):
                            if kc > 0:
                                t.wait_ge(ss, s_exp(h, qs, kc - 1))
                            t.matmul(pS[:, :], kr[:, kc * 128:(kc + 1) * 128],
                                     qr[:, qs * 512:(qs + 1) * 512], start=True, stop=False)
                            t.matmul(pS[:, :], ki[:, kc * 128:(kc + 1) * 128],
                                     qi[:, qs * 512:(qs + 1) * 512], start=False, stop=False)
                            t.matmul(pS[:, :], ones_row[0:1, 0:128], negU[:, :],
                                     start=False, stop=True).then_inc(stn, 1)
                            diag = kc >= qs * 4
                            if diag:
                                t.wait_ge(sgp, g_mask(h, qs, kc - qs * 4))
                                rhs = Ptm
                            else:
                                t.wait_ge(ss, s_exp(h, qs, kc))
                                rhs = Pt
                            t.matmul(pO[:, :], vt[kc][:, h * 64:(h + 1) * 64], rhs[:, :],
                                     start=(kc == 0), stop=(kc == KCS[qs] - 1))
                            t.matmul(pd1[:, :], ones_col[:, :], rhs[:, :],
                                     start=(kc == 0), stop=(kc == KCS[qs] - 1)).then_inc(stn, 1)
                        t.wait_ge(sv, v_den(h, qs))
                        t.matmul(pbc[:, :], ones_row[0:1, 0:64], dnb[:, :],
                                 start=True, stop=True).then_inc(stn, 1)
    return nc


def run_C(x1_full, w_attn, delta):
    nc = build_C()
    waT = np.ascontiguousarray(w_attn.T)
    in_maps = []
    for c in range(8):
        b, hg = c // 2, c % 2
        s = hg * 512
        wsl = np.concatenate([waT[:, s:s + 512], waT[:, 1024 + s:1024 + s + 512],
                              waT[:, 2048 + s:2048 + s + 512]], axis=1)
        in_maps.append({
            "xb": np.ascontiguousarray(x1_full[b]),
            "wqkv": np.ascontiguousarray(wsl),
            "delta": np.ascontiguousarray(delta[hg * 8:(hg + 1) * 8]),
        })
    res = run_bass_kernel_spmd(nc, in_maps, core_ids=list(range(8)))
    out = np.zeros((B, 1024, L), np.float32)
    for c in range(8):
        b, hg = c // 2, c % 2
        out[b, hg * 512:(hg + 1) * 512, :] = res.results[c]["o1"]
    return out


def build_D():
    nc = bass.Bass("TRN2", target_bir_lowering=False, debug=False)
    xr_d = nc.dram_tensor("xr", [2048, D], F32, kind="ExternalInput").ap()
    o1r_d = nc.dram_tensor("o1r", [D, 2048], F32, kind="ExternalInput").ap()
    wproj_d = nc.dram_tensor("wproj", [D, D], F32, kind="ExternalInput").ap()
    wfc_d = nc.dram_tensor("wfc", [8, 128, 4096], F32, kind="ExternalInput").ap()
    wpr_d = nc.dram_tensor("wpr", [32, 128, 1024], F32, kind="ExternalInput").ap()
    out_d = nc.dram_tensor("out", [2048, D], F32, kind="ExternalOutput").ap()

    NPASS, NTT = 4, 4
    st = ExitStack()
    with st:
        acc = [sb(st, nc, f"acc{i}", [128, D], F32) for i in range(NTT)]
        xnf = [sb(st, nc, f"xnf{i}", [128, D], F32) for i in range(2)]
        xnb = [sb(st, nc, f"xnb{i}", [128, D], BF16) for i in range(2)]
        ln_sum = [sb(st, nc, f"ln_sum{i}", [128, 1], F32) for i in range(2)]
        ln_nm = [sb(st, nc, f"ln_nm{i}", [128, 1], F32) for i in range(2)]
        ln_rt = [sb(st, nc, f"ln_rt{i}", [128, 1], F32) for i in range(2)]
        ln_rs = [sb(st, nc, f"ln_rs{i}", [128, 1], F32) for i in range(2)]
        sqs = sb(st, nc, "sqs", [128, D], BF16)
        xnT = [sb(st, nc, f"xnT{i}", [128, 512], BF16) for i in range(8)]
        stage = [sb(st, nc, f"stage{i}", [128, 2048], F32) for i in range(2)]
        wpb = [sb(st, nc, f"wpb{i}", [128, 1024], BF16) for i in range(8)]
        o1b = [sb(st, nc, f"o1b{i}", [128, 512], BF16) for i in range(8)]
        wfcb = [sb(st, nc, f"wfcb{i}", [128, 2048], BF16) for i in range(8)]
        wprb = [sb(st, nc, f"wprb{i}", [128, 1024], BF16) for i in range(16)]
        hb = [sb(st, nc, f"hb{i}", [128, 512], BF16) for i in range(16)]
        ysb = [sb(st, nc, f"ysb{i}", [128, D], F32) for i in range(NTT)]
        rlu = [sb(st, nc, f"rlu{i}", [128, 512], F32) for i in range(2)]
        mem = sb(st, nc, "mem", [128, D], F32)
        mem2 = sb(st, nc, "mem2", [128, D], F32)
        rst = sb(st, nc, "rst", [128, D], F32)
        spks = sb(st, nc, "spks", [128, D], F32)
        spka = sb(st, nc, "spka", [128, D], F32)
        ones_tile = sb(st, nc, "ones_tile", [128, 128], BF16)
        iden = sb(st, nc, "iden", [128, 128], BF16)
        epsln = sb(st, nc, "epsln", [128, 1], F32)

        ptrT = ps(st, nc, "ptrT", [128, 1024], BF16)
        pproj = [ps(st, nc, f"pproj{i}", [128, 512], F32) for i in range(2)]
        ph = [ps(st, nc, f"ph{i}", [128, 512], F32) for i in range(2)]
        py = [ps(st, nc, f"py{i}", [128, 512], F32) for i in range(2)]

        sld = sem(st, nc, "sld")
        sv = sem(st, nc, "sv")
        ss = sem(st, nc, "ss")
        stn = sem(st, nc, "stn")
        sgp = sem(st, nc, "sgp")
        sst = sem(st, nc, "sst")
        svr = sem(st, nc, "svr")

        def mk_loads2(p):
            ls = []
            tb = p * 512
            for tt in range(NTT):
                ls.append((xr_d[tb + tt * 128: tb + (tt + 1) * 128, :], None, acc[tt], D))
            for dc in range(8):
                ls.append((wproj_d[dc * 128:(dc + 1) * 128, :], 'st', wpb[dc], 1024))
            for dc in range(8):
                ls.append((o1r_d[dc * 128:(dc + 1) * 128, tb:tb + 512], 'st', o1b[dc], 512))
            for fh in range(2):
                for dc in range(8):
                    ls.append((wfc_d[dc, :, fh * 2048:(fh + 1) * 2048], 'st', wfcb[dc], 2048))
                for fc in range(16):
                    ls.append((wpr_d[fh * 16 + fc, :, :], 'st', wprb[fc], 1024))
            return ls

        LOADS_PASS = 4 + 8 + 8 + 2 * 24

        def ld_done(p, i):   return 16 * (LOADS_PASS * p + i + 1)
        def cast_done(p, i): return 2 + LOADS_PASS * p + i + 1
        IX_X0, IX_WP, IX_O1 = 0, 4, 12
        def ix_fh(fh):  return 20 + fh * 24

        V_PASS = 12 + 8 + 48 + 4

        def v_xc(p, tt):   return 1 + V_PASS * p + 3 * tt + 1
        def v_xnb(p, tt):  return 1 + V_PASS * p + 3 * tt + 2
        def v_xnT(p, tt):  return 1 + V_PASS * p + 3 * tt + 3
        def v_ln_all(p):   return 1 + V_PASS * p + 12
        def v_d2(p, k):    return v_ln_all(p) + k + 1
        def v_h(p, fh, fc): return v_ln_all(p) + 8 + fh * 24 + fc + 1
        def v_y(p, fh, k):  return v_ln_all(p) + 8 + fh * 24 + 16 + k + 1
        def v_snn(p, tt):  return v_ln_all(p) + 8 + 48 + tt + 1

        S_PASS = 4 + 32
        def s_ln(p, tt):   return S_PASS * p + tt + 1
        def s_h(p, fh, fc): return S_PASS * p + 4 + fh * 16 + fc + 1

        T_PASS = 4 + 8 + 48
        def t_tr(p, tt):   return T_PASS * p + tt + 1
        def t_d2(p, k):    return T_PASS * p + 4 + k + 1
        def t_h(p, fh, fc): return T_PASS * p + 12 + fh * 24 + fc + 1
        def t_y(p, fh, k):  return T_PASS * p + 12 + fh * 24 + 16 + k + 1

        @block_holder(nc)
        def build(block):
            @block.sync
            def _(sync):
                for p in range(NPASS):
                    ls = mk_loads2(p)
                    if p >= 1:
                        sync.wait_ge(sst, 16 * 4 * p)
                    for i, (src, kind, dst, w) in enumerate(ls):
                        gi = LOADS_PASS * p + i
                        if kind is None:
                            sync.dma_start(out=dst[:, :], in_=src).then_inc(sld, 16)
                        else:
                            if gi >= 2:
                                sync.wait_ge(sgp, 2 + gi - 2 + 1)
                            sync.dma_start(out=stage[gi % 2][:, 0:w], in_=src).then_inc(sld, 16)

            @block.gpsimd
            def _(g):
                g.memset(ones_tile[:], 1.0).then_inc(sgp, 1)
                g.affine_select(iden[:], ones_tile[:], [[1, 128]], ALU.is_equal, 0.0,
                                base=0, channel_multiplier=-1).then_inc(sgp, 1)
                for p in range(NPASS):
                    ls = mk_loads2(p)
                    for i, (src, kind, dst, w) in enumerate(ls):
                        gi = LOADS_PASS * p + i
                        if i == IX_WP and p >= 1:
                            g.wait_ge(stn, t_d2(p - 1, 7))
                        for fh in range(2):
                            if i == ix_fh(fh):
                                if p + fh > 0:
                                    pp, pf = (p, 0) if fh == 1 else (p - 1, 1)
                                    g.wait_ge(stn, t_y(pp, pf, 7))
                        g.wait_ge(sld, 16 * (gi + 1))
                        if kind is None:
                            g.engine_nop().then_inc(sgp, 1)
                        else:
                            g.tensor_copy(dst[:, 0:w], stage[gi % 2][:, 0:w]).then_inc(sgp, 1)

            @block.vector
            def _(v):
                nvr = [0]

                def vguard(ins):
                    ins.then_inc(svr, 1)
                    nvr[0] += 1
                    v.wait_ge(svr, nvr[0])

                v.memset(epsln[:], LN_EPS).then_inc(sv, 1)
                for p in range(NPASS):
                    for tt in range(NTT):
                        bi = tt % 2
                        v.wait_ge(sgp, cast_done(p, IX_X0 + tt))
                        vguard(v.tensor_reduce(ln_sum[bi][:, :], acc[tt][:, :], axis=AX.X, op=ALU.add))
                        vguard(v.tensor_scalar(ln_nm[bi][:, :], ln_sum[bi][:, :], -1.0 / D, None, ALU.mult))
                        v.tensor_scalar(xnf[bi][:, :], acc[tt][:, :], ln_nm[bi][:, :], None, ALU.add).then_inc(sv, 1)
                        v.wait_ge(ss, s_ln(p, tt))
                        vguard(v.reciprocal(ln_rs[bi][:, :], ln_rt[bi][:, :]))
                        v.tensor_scalar(xnb[bi][:, :], xnf[bi][:, :], ln_rs[bi][:, :], None, ALU.mult).then_inc(sv, 1)
                        v.wait_ge(stn, t_tr(p, tt))
                        for dc in range(8):
                            ins = v.tensor_copy(xnT[dc][:, tt * 128:(tt + 1) * 128],
                                                ptrT[:, dc * 128:(dc + 1) * 128])
                        ins.then_inc(sv, 1)
                    for k in range(8):
                        tt, dh = k // 2, k % 2
                        v.wait_ge(stn, t_d2(p, k))
                        v.scalar_tensor_tensor(acc[tt][:, dh * 512:(dh + 1) * 512],
                                               pproj[k % 2][:, :], 1.0,
                                               acc[tt][:, dh * 512:(dh + 1) * 512],
                                               ALU.mult, ALU.add).then_inc(sv, 1)
                    for fh in range(2):
                        for fc in range(16):
                            v.wait_ge(stn, t_h(p, fh, fc))
                            v.tensor_scalar(rlu[fc % 2][:, :], ph[fc % 2][:, :], 0.0, None,
                                            ALU.max).then_inc(sv, 1)
                        for k in range(8):
                            tsub, dh = k // 2, k % 2
                            v.wait_ge(stn, t_y(p, fh, k))
                            dstv = ysb[tsub][:, dh * 512:(dh + 1) * 512]
                            if fh == 0:
                                v.tensor_copy(dstv, py[k % 2][:, :]).then_inc(sv, 1)
                            else:
                                v.scalar_tensor_tensor(dstv, py[k % 2][:, :], 1.0, dstv,
                                                       ALU.mult, ALU.add).then_inc(sv, 1)
                    for tt in range(NTT):
                        v.tensor_copy(mem[:, :], ysb[tt][:, :])
                        v.tensor_scalar(spka[:, :], mem[:, :], 1.0, None, ALU.is_gt)
                        for step in range(4):
                            v.tensor_scalar(rst[:, :], mem[:, :], 1.0, None, ALU.is_gt)
                            v.scalar_tensor_tensor(mem2[:, :], mem[:, :], 0.9, ysb[tt][:, :],
                                                   ALU.mult, ALU.add)
                            v.tensor_tensor(mem[:, :], mem2[:, :], rst[:, :], ALU.subtract)
                            v.tensor_scalar(spks[:, :], mem[:, :], 1.0, None, ALU.is_gt)
                            v.tensor_tensor(spka[:, :], spka[:, :], spks[:, :], ALU.add)
                        vguard(v.engine_nop())
                        v.scalar_tensor_tensor(acc[tt][:, :], spka[:, :], 0.2, acc[tt][:, :],
                                               ALU.mult, ALU.add).then_inc(sv, 1)

            @block.scalar
            def _(s):
                for p in range(NPASS):
                    for tt in range(NTT):
                        bi = tt % 2
                        s.wait_ge(sv, v_xc(p, tt))
                        s.activation(sqs[:, :], xnf[bi][:, :], AF.Square, accum_out=ln_sum[bi][:, :])
                        s.activation(ln_rt[bi][:, :], ln_sum[bi][:, :], AF.Sqrt,
                                     scale=1.0 / D, bias=epsln[:, :]).then_inc(ss, 1)
                    for fh in range(2):
                        for fc in range(16):
                            s.wait_ge(sv, v_h(p, fh, fc))
                            s.activation(hb[fc][:, :], rlu[fc % 2][:, :], AF.Square).then_inc(ss, 1)
                    for tt in range(NTT):
                        s.wait_ge(sv, v_snn(p, tt))
                        s.dma_start(out=out_d[p * 512 + tt * 128:p * 512 + (tt + 1) * 128, :],
                                    in_=acc[tt][:, :]).then_inc(sst, 16)
                s.wait_ge(sst, 16 * 16)

            @block.tensor
            def _(t):
                t.wait_ge(sgp, 2)
                for p in range(NPASS):
                    for tt in range(NTT):
                        gi = p * NTT + tt
                        t.wait_ge(sv, v_xnb(p, tt))
                        if gi >= 1:
                            g2 = gi - 1
                            t.wait_ge(sv, v_xnT(g2 // NTT, g2 % NTT))
                        for dc in range(8):
                            ins = t.transpose(ptrT[:, dc * 128:(dc + 1) * 128],
                                              xnb[tt % 2][:, dc * 128:(dc + 1) * 128], iden[:, :])
                        ins.then_inc(stn, 1)
                    t.wait_ge(sgp, cast_done(p, IX_O1 + 7))
                    for k in range(8):
                        tt, dh = k // 2, k % 2
                        if k == 0:
                            t.wait_ge(sv, v_ln_all(p))
                        if k >= 2:
                            t.wait_ge(sv, v_d2(p, k - 2))
                        elif p >= 1:
                            t.wait_ge(sv, v_d2(p - 1, 6 + k))
                        for dc in range(8):
                            ins = t.matmul(pproj[k % 2][:, :], o1b[dc][:, tt * 128:(tt + 1) * 128],
                                           wpb[dc][:, dh * 512:(dh + 1) * 512],
                                           start=(dc == 0), stop=(dc == 7))
                        ins.then_inc(stn, 1)
                    for fh in range(2):
                        t.wait_ge(sgp, cast_done(p, ix_fh(fh) + 23))
                        for fc in range(16):
                            if fc >= 2:
                                t.wait_ge(sv, v_h(p, fh, fc - 2))
                            elif p + fh > 0:
                                pp, pf = (p, 0) if fh == 1 else (p - 1, 1)
                                t.wait_ge(sv, v_h(pp, pf, 14 + fc))
                            for dc in range(8):
                                ins = t.matmul(ph[fc % 2][:, :], wfcb[dc][:, fc * 128:(fc + 1) * 128],
                                               xnT[dc][:, :], start=(dc == 0), stop=(dc == 7))
                            ins.then_inc(stn, 1)
                        for k in range(8):
                            tsub, dh = k // 2, k % 2
                            if k == 0:
                                t.wait_ge(ss, s_h(p, fh, 15))
                            if k >= 2:
                                t.wait_ge(sv, v_y(p, fh, k - 2))
                            elif p + fh > 0:
                                pp, pf = (p, 0) if fh == 1 else (p - 1, 1)
                                t.wait_ge(sv, v_y(pp, pf, 6 + k))
                            for fc in range(16):
                                ins = t.matmul(py[k % 2][:, :], hb[fc][:, tsub * 128:(tsub + 1) * 128],
                                               wprb[fc][:, dh * 512:(dh + 1) * 512],
                                               start=(fc == 0), stop=(fc == 15))
                            ins.then_inc(stn, 1)
    return nc


def run_D(x1_rows, o1_full, w_proj, W_fc, W_proj2):
    nc = build_D()
    wprojT = np.ascontiguousarray(w_proj.T)
    wfc = np.ascontiguousarray(W_fc.T.reshape(8, 128, 4096))
    wpr = np.ascontiguousarray(W_proj2.T.reshape(32, 128, 1024))
    in_maps = []
    for c in range(8):
        b, hf = c // 2, c % 2
        in_maps.append({
            "xr": np.ascontiguousarray(x1_rows[c]),
            "o1r": np.ascontiguousarray(o1_full[b][:, hf * 2048:(hf + 1) * 2048]),
            "wproj": wprojT, "wfc": wfc, "wpr": wpr,
        })
    res = run_bass_kernel_spmd(nc, in_maps, core_ids=list(range(8)))
    return [res.results[c]["out"] for c in range(8)]


def kernel(L0_attn_x, L0_attn_W_qkv, L0_attn_b_qkv, L0_attn_W_out, L0_attn_b_out,
           L0_ffn_Wg, L0_ffn_routing_bias, L0_ffn_W1, L0_ffn_W2, L0_ffn_W3,
           L0_ffn_Ws1, L0_ffn_Ws2, L0_ffn_Ws3,
           L1_attn_w_attn, L1_attn_w_proj, L1_attn_delta, L1_ffn_W_fc, L1_ffn_W_proj):
    x = np.ascontiguousarray(np.asarray(L0_attn_x, dtype=np.float32))
    ola = run_A(x, np.asarray(L0_attn_W_qkv), np.asarray(L0_attn_b_qkv))
    x_rows = [np.ascontiguousarray(x[c // 2, (c % 2) * 2048:(c % 2 + 1) * 2048])
              for c in range(8)]
    x1_rows = run_B(x_rows, ola, np.asarray(L0_attn_W_out), np.asarray(L0_attn_b_out),
                    np.asarray(L0_ffn_Wg), np.asarray(L0_ffn_routing_bias),
                    np.asarray(L0_ffn_W1), np.asarray(L0_ffn_W2), np.asarray(L0_ffn_W3),
                    np.asarray(L0_ffn_Ws1), np.asarray(L0_ffn_Ws2), np.asarray(L0_ffn_Ws3))
    x1_full = np.stack([np.concatenate([x1_rows[2 * b], x1_rows[2 * b + 1]], axis=0)
                        for b in range(B)])
    o1 = run_C(x1_full, np.asarray(L1_attn_w_attn), np.asarray(L1_attn_delta))
    rows = run_D(x1_rows, o1, np.asarray(L1_attn_w_proj), np.asarray(L1_ffn_W_fc),
                 np.asarray(L1_ffn_W_proj))
    return np.stack([np.concatenate([rows[2 * b], rows[2 * b + 1]], axis=0)
                     for b in range(B)])


def _headslice(W, hg):
    s = hg * 512
    return np.ascontiguousarray(np.concatenate(
        [W[:, s:s + 512], W[:, 1024 + s:1024 + s + 512], W[:, 2048 + s:2048 + s + 512]], axis=1))


def run_A(x, W_qkv, b_qkv):
    nc = build_A()
    in_maps = []
    for c in range(8):
        b, hg = c // 2, c % 2
        bs = np.concatenate([b_qkv[hg * 512:(hg + 1) * 512],
                             b_qkv[1024 + hg * 512:1024 + (hg + 1) * 512],
                             b_qkv[2048 + hg * 512:2048 + (hg + 1) * 512]])
        in_maps.append({
            "xb": np.ascontiguousarray(x[b]),
            "wqkv": _headslice(W_qkv, hg),
            "brow": np.ascontiguousarray(bs.reshape(1, 1536)),
        })
    res = run_bass_kernel_spmd(nc, in_maps, core_ids=list(range(8)))
    out = np.zeros((B, 1024, L), np.float32)
    for c in range(8):
        b, hg = c // 2, c % 2
        out[b, hg * 512:(hg + 1) * 512, :] = res.results[c]["ola"]
    return out
